# version 04
# baseline (speedup 1.0000x reference)
import contextlib
import os
import numpy as np
import ml_dtypes
import concourse.bass as bass
import concourse.mybir as mybir
from concourse.bass_utils import run_bass_kernel_spmd

F32 = mybir.dt.float32
F16 = mybir.dt.float16
BF16 = mybir.dt.bfloat16
I16 = mybir.dt.int16
U8 = mybir.dt.uint8
AF = mybir.ActivationFunctionType
ALU = mybir.AluOpType
AX = mybir.AxisListType

B, S, D, H, DH = 2, 8192, 1024, 16, 64
HID = 4096
NTOK = 2048
NT_OWN = 16
NT_ALL = 64
LN_EPS = 1e-5
KB = 1024
MAGIC_C = 12582912.0


class Sem:
    def __init__(self, h):
        self.h = h
        self.n = 0

    def tick(self, k=1):
        self.n += k
        return self.n


KSTAGE = int(os.environ.get("KSTAGE", "99"))
KTILES = int(os.environ.get("KTILES", "64"))
KPART = int(os.environ.get("KPART", "9"))


def build_program():
    nc = bass.Bass("TRN2", target_bir_lowering=False, debug=False, detect_race_conditions=False)

    x_in = nc.dram_tensor("x", [S, D], F32, kind="ExternalInput").ap()
    xq_in = nc.dram_tensor("xq", [NTOK, D], F32, kind="ExternalInput").ap()
    wq_in = nc.dram_tensor("wq", [D, D], F32, kind="ExternalInput").ap()
    wkv_in = nc.dram_tensor("wkv", [D, 2 * D], F32, kind="ExternalInput").ap()
    wout_in = nc.dram_tensor("wout", [D, D], F32, kind="ExternalInput").ap()
    bout_in = nc.dram_tensor("bout", [D], F32, kind="ExternalInput").ap()
    scale_in = nc.dram_tensor("scale", [H], F32, kind="ExternalInput").ap()
    ln1g_in = nc.dram_tensor("ln1g", [D], F32, kind="ExternalInput").ap()
    ln1b_in = nc.dram_tensor("ln1b", [D], F32, kind="ExternalInput").ap()
    w1_in = nc.dram_tensor("w1", [HID, D], F32, kind="ExternalInput").ap()
    b1_in = nc.dram_tensor("b1", [HID], F32, kind="ExternalInput").ap()
    ln2g_in = nc.dram_tensor("ln2g", [HID], F32, kind="ExternalInput").ap()
    ln2b_in = nc.dram_tensor("ln2b", [HID], F32, kind="ExternalInput").ap()
    w2_in = nc.dram_tensor("w2", [D, HID], F32, kind="ExternalInput").ap()
    b2_in = nc.dram_tensor("b2", [D], F32, kind="ExternalInput").ap()
    idf_in = nc.dram_tensor("idf", [128, 128], F16, kind="ExternalInput").ap()
    idb_in = nc.dram_tensor("idb", [128, 128], BF16, kind="ExternalInput").ap()
    out_hbm = nc.dram_tensor("out", [NTOK, D], F32, kind="ExternalOutput").ap()

    kt_scr = nc.dram_tensor("kt_scr", [H // 2, 8, 128, 1024], F16).ap()
    v_scr = nc.dram_tensor("v_scr", [NT_ALL, 128, 1024], F16).ap()
    h_scr = nc.dram_tensor("h_scr", [NT_OWN, 128, HID], F16).ap()
    sc_scr = nc.dram_tensor("sc_scr", [1024], F32).ap()

    st = contextlib.ExitStack()
    smalls = {}
    for name, shp, dt in [
        ("cvec", [128, H], F32), ("nbias", [128, H], F32),
        ("scrow", [1, H], F32), ("crow", [1, H], F32),
        ("ones64", [1, 64], F16),
        ("idfs", [128, 128], F16), ("idbs", [128, 128], BF16),
        ("boutr", [128, D], F32),
        ("zero1", [1, 1], F32), ("zero128", [128, 1], F32),
        ("one128", [128, 1], F32), ("neghalf", [128, 1], F32),
        ("eps128", [128, 1], F32),
        ("s1", [128, 1], F32), ("s2", [128, 1], F32), ("sa", [128, 1], F32),
        ("sb", [128, 1], F32), ("sc2", [128, 1], F32), ("sd", [128, 1], F32),
        ("rl1", [1, 1024], F32), ("rl16", [1, 1024], F16),
        ("rlb", [64, 1024], F32),
        ("means", [128, 16], F32), ("rstds", [128, 16], F32),
        ("srow", [128, 16], F32), ("mwsr", [128, 16], F32), ("m16b", [128, 16], F32),
        ("wtot", [1, 64], F32), ("rowred", [1, 4], F32),
        ("thr", [128, 1], F32), ("nthr", [128, 1], F32), ("mwr", [128, 1], F32),
        ("thr2", [128, 1], F32), ("nthr2", [128, 1], F32), ("mwr2", [128, 1], F32),
        ("tiny1", [128, 1], F32), ("se1", [128, 1], F32), ("v16", [128, 16], F32),
        ("wcol", [128, 1], F32), ("wcolb", [128, 1], F32), ("m16d", [128, 16], F32),
        ("negep5", [128, 1], F32),
        ("kap1h", [128, 1], F32), ("kap1", [128, 1], F32),
        ("sc1", [128, 1], F32), ("sc3", [128, 1], F32),
        ("wab8", [128, 128], F32), ("wab8L", [128, 128], F32),
        ("m16c", [128, 16], F32), ("m16e", [128, 16], F32),
    ]:
        smalls[name] = st.enter_context(nc.sbuf_tensor(name, shp, dt))[:]
    SM = type("SM", (), smalls)

    ARENA_B = 190 * KB
    arena_t = st.enter_context(nc.sbuf_tensor("arena", [128, ARENA_B], U8))
    AR = arena_t[:]

    def at(off_b, cols, dt):
        esz = {F32: 4, F16: 2, BF16: 2, I16: 2, U8: 1}[dt]
        assert off_b % esz == 0 and off_b + cols * esz <= ARENA_B, (off_b, cols, dt)
        return AR[:, off_b:off_b + cols * esz].bitcast(dt)

    ps_t = st.enter_context(nc.psum_tensor("PS", [128, 4096], F32))
    PS = ps_t[:]

    def bank(b, r0=0, r1=128, o=0, n=512):
        return PS[r0:r1, 512 * b + o: 512 * b + o + n]

    def banks(b0, nb, r0=0, r1=128):
        return PS[r0:r1, 512 * b0: 512 * (b0 + nb)]

    sems = {n: Sem(st.enter_context(nc.semaphore(n))) for n in
            ("sS", "sI", "sWc", "sLa", "sLb", "sVa", "sVb", "sK", "sHa", "sHb",
             "sOa", "sOb", "sPE", "sA", "sV")}
    sS, sI, sWc = (sems[k] for k in ("sS", "sI", "sWc"))
    sK = sems["sK"]
    sOs = [sems["sOa"], sems["sOb"]]
    sL = [sems["sLa"], sems["sLb"]]
    sVs = [sems["sVa"], sems["sVb"]]
    sH = [sems["sHa"], sems["sHb"]]
    sPE, sA, sV = (sems[k] for k in ("sPE", "sA", "sV"))

    q = {e: [] for e in ("pe", "act", "dve", "sp", "pool")}

    def E(eng, fn):
        q[eng].append(fn)

    def W(eng, sem, tok):
        if tok is None:
            return
        assert tok <= sem.n, f"wait on future token {tok}>{sem.n}"
        q[eng].append(lambda e, s=sem.h, t=tok: e.wait_ge(s, t))

    def _finish():
        for sm in sems.values():
            if sm.n:
                W("sp", sm, sm.n)
        with nc.Block() as block:
            @block.sync
            def _(e):
                for fn in q["sp"]:
                    fn(e)

            @block.gpsimd
            def _(e):
                for fn in q["pool"]:
                    fn(e)

            @block.tensor
            def _(e):
                for fn in q["pe"]:
                    fn(e)

            @block.scalar
            def _(e):
                for fn in q["act"]:
                    fn(e)

            @block.vector
            def _(e):
                for fn in q["dve"]:
                    fn(e)
        st.close()
        return nc

    def transpose_waves(nchunks, lhs_fn, evac_fn, ident, prev_tok, pre_pe_waits=()):
        t_ev = prev_tok
        for w0 in range(0, nchunks, 4):
            nw = min(4, nchunks - w0)
            W("pe", sV, t_ev)
            for eng, sem, tok in pre_pe_waits if w0 == 0 else ():
                W(eng, sem, tok)
            def _tr(e, w0=w0, nw=nw):
                for j in range(nw):
                    ins = e.matmul(bank(j, o=0, n=128), lhs_fn(w0 + j), ident,
                                   start=True, stop=True)
                ins.then_inc(sPE.h, 1)
            t_tr = sPE.tick()
            E("pe", _tr)
            W("dve", sPE, t_tr)
            def _ev(e, w0=w0, nw=nw):
                for j in range(nw):
                    ins = evac_fn(e, w0 + j, bank(j, o=0, n=128))
                ins.then_inc(sV.h, 1)
            t_ev = sV.tick()
            E("dve", _ev)
        return t_ev

    x1 = at(0, 16 * 1024, F16)
    aoT = at(32 * KB, 8 * 2048, BF16)
    qt = at(64 * KB, 16 * 1024, F16)

    def x1_t(t):
        return x1[:, 1024 * t:1024 * (t + 1)]

    def aoT_p(p):
        return aoT[:, 2048 * p:2048 * (p + 1)]

    def qt_h(h):
        return qt[:, 1024 * h:1024 * (h + 1)]

    t_idf = sI.tick(16)
    E("sp", lambda e: e.dma_start(out=SM.idfs, in_=idf_in).then_inc(sI.h, 16))
    t_idb = sI.tick(16)
    E("sp", lambda e: e.dma_start(out=SM.idbs, in_=idb_in).then_inc(sI.h, 16))
    t_boutr = sS.tick(16)
    E("sp", lambda e: e.dma_start(
        out=SM.boutr, in_=bout_in[None, :].broadcast_to((128, D))).then_inc(sS.h, 16))
    t_scrow = sS.tick(16)
    E("sp", lambda e: e.dma_start(out=SM.scrow, in_=scale_in[None, :]).then_inc(sS.h, 16))

    t_consts = sV.tick()
    def _p0consts(e):
        e.memset(SM.ones64, 1.0)
        e.memset(SM.zero1, 0.0)
        e.memset(SM.zero128, 0.0)
        e.memset(SM.one128, 1.0)
        e.memset(SM.neghalf, -0.5)
        e.memset(SM.tiny1, 1e-24)
        e.memset(SM.negep5, -1e-5)
        e.memset(SM.kap1h, 0.4999995)
        e.memset(SM.eps128, LN_EPS).then_inc(sV.h, 1)
    E("dve", _p0consts)

    W("act", sS, t_scrow)
    W("act", sV, t_consts)
    t_crow = sA.tick()
    E("act", lambda e: e.activation(SM.crow, SM.scrow, AF.Exp, bias=SM.zero1,
                                    scale=SM.one128[0:1, :]).then_inc(sA.h, 1))
    W("dve", sA, t_crow)
    t_kap = sV.tick()
    E("dve", lambda e: e.tensor_scalar(
        SM.kap1, SM.kap1h, MAGIC_C, MAGIC_C, ALU.add, ALU.subtract).then_inc(sV.h, 1))
    W("sp", sA, t_crow)
    t_cw = sS.tick(16)
    E("sp", lambda e: e.dma_start(out=sc_scr[0:16], in_=SM.crow[0:1, :]).then_inc(sS.h, 16))
    W("sp", sS, t_cw)
    t_cvec = sS.tick(16)
    E("sp", lambda e: e.dma_start(
        out=SM.cvec, in_=sc_scr[None, 0:16].broadcast_to((128, 16))).then_inc(sS.h, 16))
    W("dve", sS, t_cvec)
    t_nbias = sV.tick()
    E("dve", lambda e: e.tensor_scalar_mul(SM.nbias, SM.cvec, -1.0).then_inc(sV.h, 1))

    wkv_sb = at(96 * KB, 8 * 2048, BF16)
    xb = [at(128 * KB, 1024, BF16), at(130 * KB, 1024, BF16)]
    xT = [at(132 * KB, 1024, BF16), at(134 * KB, 1024, BF16)]
    ksq = at(136 * KB, 1024, F32)
    ssk = at(140 * KB, 16, F32)
    rnk = at(141 * KB, 16, F32)
    kn16 = at(142 * KB, 1024, F16)
    vst = [at(144 * KB, 1024, F16), at(146 * KB, 1024, F16)]
    kst = at(148 * KB, 8 * 1024, F16)
    kf32 = at(164 * KB, 1024, F32)

    t_wkv = None
    for c in range(8):
        t_wkv = sWc.tick(16)
        E("pool", lambda e, c=c: e.dma_start(
            out=wkv_sb[:, 2048 * c:2048 * (c + 1)],
            in_=wkv_in[128 * c:128 * (c + 1), :]).then_inc(sWc.h, 16))

    W("pe", sI, t_idb)
    W("pe", sWc, t_wkv)

    tok_trdone = [None, None]
    tok_xT = [None, None]
    tok_kfree = None
    tok_vfree = None
    tok_vst_dma = [None, None]
    tok_kst_dma = None
    tok_kst_w = None
    t_rnk_last = None
    t_kn_prev = None
    t_trev = None

    for i in range(KTILES):
        bi = i % 2
        W("pool", sPE, tok_trdone[bi])
        t_x = sL[bi].tick(16)
        E("pool", lambda e, i=i, bi=bi: e.dma_start(
            out=xb[bi], in_=x_in[128 * i:128 * (i + 1), :]).then_inc(sL[bi].h, 16))
        W("pe", sL[bi], t_x)
        t_trev = transpose_waves(
            8, lambda c, bi=bi: xb[bi][:, 128 * c:128 * (c + 1)],
            lambda e, c, pv, bi=bi: e.tensor_copy(xT[bi][:, 128 * c:128 * (c + 1)], pv),
            SM.idbs, t_trev)
        tok_trdone[bi] = sPE.n
        tok_xT[bi] = t_trev
        if KPART <= 1:
            continue
        W("pe", sV, tok_xT[bi])
        W("pe", sV, tok_vfree)
        def _mm_v(e, bi=bi):
            for nb in (2, 3):
                for c in range(8):
                    ins = e.matmul(bank(4 + nb), xT[bi][:, 128 * c:128 * (c + 1)],
                                   wkv_sb[:, 2048 * c + 512 * nb:2048 * c + 512 * (nb + 1)],
                                   start=(c == 0), stop=(c == 7))
            ins.then_inc(sPE.h, 1)
        t_vmm = sPE.tick()
        E("pe", _mm_v)
        W("pe", sV, tok_kfree)
        def _mm_k(e, bi=bi):
            for nb in (0, 1):
                for c in range(8):
                    ins = e.matmul(bank(4 + nb), xT[bi][:, 128 * c:128 * (c + 1)],
                                   wkv_sb[:, 2048 * c + 512 * nb:2048 * c + 512 * (nb + 1)],
                                   start=(c == 0), stop=(c == 7))
            ins.then_inc(sPE.h, 1)
        t_kmm = sPE.tick()
        E("pe", _mm_k)
        if KPART <= 2:
            continue
        W("dve", sPE, t_vmm)
        W("dve", sVs[bi], tok_vst_dma[bi])
        t_vev = sV.tick()
        E("dve", lambda e, bi=bi: e.tensor_copy(vst[bi], banks(6, 2)).then_inc(sV.h, 1))
        tok_vfree = t_vev
        if KPART <= 3:
            continue
        W("act", sPE, t_kmm)
        W("act", sV, t_kn_prev)
        t_ksq = sA.tick()
        def _ksq(e):
            e.activation(ksq, banks(4, 2), AF.Square, bias=SM.zero128,
                         scale=SM.one128)
            e.copy(kf32, banks(4, 2)).then_inc(sA.h, 1)
        E("act", _ksq)
        W("dve", sA, t_ksq)
        sV.tick()
        t_ssk = sV.tick()
        def _ssk(e):
            e.tensor_reduce(rnk[:, 0:8].rearrange("p (g o) -> p g o", o=1),
                            ksq[:, 0:512].rearrange("p (g i) -> p g i", i=64),
                            AX.X, ALU.add).then_inc(sV.h, 1)
            e.tensor_reduce(rnk[:, 8:16].rearrange("p (g o) -> p g o", o=1),
                            ksq[:, 512:1024].rearrange("p (g i) -> p g i", i=64),
                            AX.X, ALU.add).then_inc(sV.h, 1)
        E("dve", _ssk)
        W("act", sV, t_ssk)
        t_rnk = sA.tick()
        def _rnk(e):
            e.activation(ksq[:, 0:16], rnk, AF.Ln, bias=SM.tiny1, scale=SM.one128)
            e.activation(rnk, ksq[:, 0:16], AF.Exp, bias=SM.zero128,
                         scale=SM.neghalf).then_inc(sA.h, 1)
        E("act", _rnk)
        t_rnk_last = t_rnk
        W("dve", sA, t_rnk)
        sV.tick()
        t_kn = sV.tick()
        def _kn(e):
            for hh in range(2):
                cs = slice(512 * hh, 512 * (hh + 1))
                e.tensor_tensor(kn16[:, cs].rearrange("p (g i) -> p g i", i=64),
                                kf32[:, cs].rearrange("p (g i) -> p g i", i=64),
                                rnk[:, 8 * hh:8 * (hh + 1)].rearrange(
                                    "p (g o) -> p g o", o=1).broadcast_to((128, 8, 64)),
                                ALU.mult).then_inc(sV.h, 1)
        E("dve", _kn)
        t_kn_prev = t_kn
        tok_kfree = t_ksq
        if KPART <= 4:
            continue
        W("pe", sV, t_kn)
        W("dve", sK, tok_kst_dma)
        slot = i % 8
        t_trev = transpose_waves(
            8, lambda c: kn16[:, 128 * c:128 * (c + 1)],
            lambda e, c, pv, slot=slot: e.tensor_copy(
                kst[:, 1024 * c + 128 * slot:1024 * c + 128 * (slot + 1)], pv),
            SM.idfs, t_trev)
        tok_kst_w = t_trev
        if KPART <= 5:
            continue
        W("sp", sV, t_vev)
        tok_vst_dma[bi] = sVs[bi].tick(16)
        E("sp", lambda e, i=i, bi=bi: e.dma_start(
            out=v_scr[i], in_=vst[bi]).then_inc(sVs[bi].h, 16))
        if i % 8 == 7:
            W("sp", sV, tok_kst_w)
            for p in range(8):
                tok_kst_dma = sK.tick(16)
                E("sp", lambda e, p=p, blk=i // 8: e.dma_start(
                    out=kt_scr[p, blk],
                    in_=kst[:, 1024 * p:1024 * (p + 1)]).then_inc(sK.h, 16))
    t_p1kv_dma = tok_kst_dma
    t_p1kv_vdma = max((t for t in tok_vst_dma if t is not None), default=None)
    if KSTAGE <= 1:
        return _finish()

    wq_sb = at(96 * KB, 8 * 1024, BF16)
    qb = [at(112 * KB, 1024, BF16), at(114 * KB, 1024, BF16)]
    qT = [at(116 * KB, 1024, BF16), at(118 * KB, 1024, BF16)]
    qsq = at(120 * KB, 1024, F32)
    ssq = at(124 * KB, 16, F32)
    rnq = at(125 * KB, 16, F32)
    qn16 = at(126 * KB, 1024, F16)

    W("pool", sK, t_p1kv_dma)
    W("pool", sVs[0], sVs[0].n)
    W("pool", sVs[1], sVs[1].n)
    t_wq = None
    for c in range(8):
        t_wq = sWc.tick(16)
        E("pool", lambda e, c=c: e.dma_start(
            out=wq_sb[:, 1024 * c:1024 * (c + 1)],
            in_=wq_in[128 * c:128 * (c + 1), :]).then_inc(sWc.h, 16))

    tok_trdone = [None, None]
    tok_qT = [None, None]
    tok_qfree = None
    t_qn_prev = t_kn_prev
    t_rnq_last = None
    t_trq_last = None
    W("pe", sWc, t_wq)
    W("pe", sA, t_rnk_last)
    t_gq_last = None
    for i in range(NT_OWN):
        bi = i % 2
        W("pool", sPE, tok_trdone[bi])
        t_gx = sL[bi].tick(16)
        t_gq_last = t_gx
        E("pool", lambda e, i=i, bi=bi: e.dma_start(
            out=qb[bi], in_=xq_in[128 * i:128 * (i + 1), :]).then_inc(sL[bi].h, 16))
        W("pe", sL[bi], t_gx)
        t_trev = transpose_waves(
            8, lambda c, bi=bi: qb[bi][:, 128 * c:128 * (c + 1)],
            lambda e, c, pv, bi=bi: e.tensor_copy(qT[bi][:, 128 * c:128 * (c + 1)], pv),
            SM.idbs, t_trev)
        tok_trdone[bi] = sPE.n
        tok_qT[bi] = t_trev
        W("pe", sV, tok_qT[bi])
        W("pe", sV, tok_qfree)
        def _mm_q(e, bi=bi):
            for nb in (0, 1):
                for c in range(8):
                    ins = e.matmul(bank(4 + nb), qT[bi][:, 128 * c:128 * (c + 1)],
                                   wq_sb[:, 1024 * c + 512 * nb:1024 * c + 512 * (nb + 1)],
                                   start=(c == 0), stop=(c == 7))
            ins.then_inc(sPE.h, 1)
        t_qmm = sPE.tick()
        E("pe", _mm_q)
        W("act", sPE, t_qmm)
        W("act", sV, t_qn_prev)
        t_qsq = sA.tick()
        def _qsqf(e):
            e.activation(qsq, banks(4, 2), AF.Square, bias=SM.zero128,
                         scale=SM.one128)
            e.copy(kf32, banks(4, 2)).then_inc(sA.h, 1)
        E("act", _qsqf)
        W("dve", sA, t_qsq)
        sV.tick()
        t_ssq = sV.tick()
        def _ssq(e):
            e.tensor_reduce(rnq[:, 0:8].rearrange("p (g o) -> p g o", o=1),
                            qsq[:, 0:512].rearrange("p (g i) -> p g i", i=64),
                            AX.X, ALU.add).then_inc(sV.h, 1)
            e.tensor_reduce(rnq[:, 8:16].rearrange("p (g o) -> p g o", o=1),
                            qsq[:, 512:1024].rearrange("p (g i) -> p g i", i=64),
                            AX.X, ALU.add).then_inc(sV.h, 1)
        E("dve", _ssq)
        W("act", sV, t_ssq)
        t_rnq = sA.tick()
        def _rnq(e):
            e.activation(qsq[:, 0:16], rnq, AF.Ln, bias=SM.tiny1, scale=SM.one128)
            e.activation(rnq, qsq[:, 0:16], AF.Exp, bias=SM.zero128,
                         scale=SM.neghalf).then_inc(sA.h, 1)
        E("act", _rnq)
        t_rnq_last = t_rnq
        W("dve", sA, t_rnq)
        sV.tick()
        t_qn = sV.tick()
        def _qn(e):
            for hh in range(2):
                cs = slice(512 * hh, 512 * (hh + 1))
                e.tensor_tensor(qn16[:, cs].rearrange("p (g i) -> p g i", i=64),
                                kf32[:, cs].rearrange("p (g i) -> p g i", i=64),
                                rnq[:, 8 * hh:8 * (hh + 1)].rearrange(
                                    "p (g o) -> p g o", o=1).broadcast_to((128, 8, 64)),
                                ALU.mult).then_inc(sV.h, 1)
        E("dve", _qn)
        t_qn_prev = t_qn
        tok_qfree = t_qsq
        W("pe", sV, t_qn)
        qq = i // 4
        r0 = 64 * (qq // 2)
        col = 512 * (qq % 2) + 128 * (i % 4)
        def _evq(e, c, pv, r0=r0, col=col):
            e.tensor_copy(qt_h(2 * c)[r0:r0 + 64, col:col + 128], pv[0:64, :])
            return e.tensor_copy(qt_h(2 * c + 1)[r0:r0 + 64, col:col + 128], pv[64:128, :])
        t_trev = transpose_waves(
            8, lambda c: qn16[:, 128 * c:128 * (c + 1)], _evq, SM.idfs, t_trev)
        t_trq_last = sPE.n
    t_p1q_done = t_trev
    if KSTAGE <= 2:
        return _finish()

    ktb = [at(96 * KB, 8192, F16), at(112 * KB, 8192, F16)]
    v65 = [at(128 * KB, 65 * 64, F16), at(137 * KB, 65 * 64, F16)]
    pt1 = at(146 * KB, 2048, F16)
    pt2 = at(150 * KB, 1024, F16)

    t_v65ones = sV.tick()
    def _v65ones(e):
        for b2i in range(2):
            ins = e.memset(v65[b2i], 1.0)
        ins.then_inc(sV.h, 1)
    W("dve", sA, t_rnq_last)
    W("dve", sL[0], sL[0].n)
    W("dve", sL[1], sL[1].n)
    W("dve", sVs[0], sVs[0].n)
    W("dve", sVs[1], sVs[1].n)
    E("dve", _v65ones)

    def load_head(h, tok_prev_use):
        b2i = h % 2
        p = h // 2
        r = 64 * (h % 2)
        sm = sH[b2i]
        W("sp", sPE, tok_prev_use)
        for half in (0, 1):
            sm.tick(16)
            E("sp", lambda e, b2i=b2i, p=p, r=r, half=half, sm=sm: e.dma_start(
                out=ktb[b2i][64 * half:64 * half + 64, :].rearrange(
                    "p (t c) -> p t c", t=8),
                in_=kt_scr[p, :, r:r + 64, :].rearrange("t p c -> p t c")).then_inc(sm.h, 16))
        t = sm.tick(16)
        E("sp", lambda e, b2i=b2i, h=h, sm=sm: e.dma_start(
            out=v65[b2i].rearrange("p (c k) -> p c k", k=65)[:, :, 0:64],
            in_=v_scr[:, :, 64 * h:64 * h + 64].rearrange("t p c -> p t c")).then_inc(sm.h, 16))
        return t

    W("sp", sK, t_p1kv_dma)
    W("sp", sVs[0], sVs[0].n)
    W("sp", sVs[1], sVs[1].n)
    W("sp", sPE, t_trq_last)
    W("sp", sV, t_p1q_done)
    W("sp", sA, t_rnq_last)
    W("sp", sV, t_v65ones)
    t_kv_ready = [None] * H
    t_kv_ready[0] = load_head(0, None)
    t_kv_ready[1] = load_head(1, None)

    sched = []
    c = 0
    while c < 64:
        if c % 3 == 0 and c + 1 < 64:
            sched.append(("T1", [c, c + 1]))
            c += 2
        else:
            sched.append(("T2", [c]))
            c += 1

    pv_a = PS[0:65, 512 * 6:512 * 7]
    pv_b = PS[0:65, 512 * 7:512 * 8]

    tok_slot_exp = {"T1": None, "T2": None}
    tok_slot_pv = {"T1": None, "T2": None}
    tok_pv_free = None
    tok_ao_evac = None
    t_exp_last = None
    t_aoT = None

    W("pe", sV, t_p1q_done)
    W("pe", sA, t_rnq_last)
    for h in range(H):
        kb2 = h % 2
        W("pe", sH[kb2], t_kv_ready[h])
        for qp in range(2):
            qa = 512 * qp
            for gi, (slot, chs) in enumerate(sched):
                sb0 = 0 if slot == "T1" else 4
                pt = pt1 if slot == "T1" else pt2
                fd = 1024 * len(chs)
                W("pe", sA, tok_slot_exp[slot])
                if gi == 0:
                    W("pe", sV, tok_pv_free)
                if slot == "T2":
                    W("pe", sV, tok_ao_evac)
                def _sims(e, chs=chs, sb0=sb0, kb2=kb2, h=h, qa=qa):
                    for j, ch in enumerate(chs):
                        e.matmul(bank(sb0 + 2 * j), ktb[kb2][0:64, 128 * ch:128 * (ch + 1)],
                                 qt_h(h)[0:64, qa:qa + 512],
                                 start=True, stop=True, tile_position=(0, 0))
                        ins = e.matmul(bank(sb0 + 2 * j + 1),
                                       ktb[kb2][64:128, 128 * ch:128 * (ch + 1)],
                                       qt_h(h)[64:128, qa:qa + 512],
                                       start=True, stop=True, tile_position=(64, 0))
                    ins.then_inc(sPE.h, 1)
                t_sim = sPE.tick()
                E("pe", _sims)
                W("act", sPE, t_sim)
                W("act", sPE, tok_slot_pv[slot])
                def _exp(e, pt=pt, sb0=sb0, fd=fd, h=h):
                    e.activation(pt[:, 0:fd], banks(sb0, fd // 512), AF.Exp,
                                 bias=SM.nbias[:, h:h + 1],
                                 scale=SM.cvec[:, h:h + 1]).then_inc(sA.h, 1)
                tok_slot_exp[slot] = sA.tick()
                t_exp_last = tok_slot_exp[slot]
                E("act", _exp)
                W("pe", sA, tok_slot_exp[slot])
                def _pv(e, chs=chs, pt=pt, kb2=kb2):
                    for j, ch in enumerate(chs):
                        e.matmul(pv_a, v65[kb2][:, 65 * ch:65 * ch + 65],
                                 pt[:, 1024 * j:1024 * j + 512],
                                 start=(ch == 0), stop=(ch == 63))
                        ins = e.matmul(pv_b, v65[kb2][:, 65 * ch:65 * ch + 65],
                                       pt[:, 1024 * j + 512:1024 * j + 1024],
                                       start=(ch == 0), stop=(ch == 63))
                    ins.then_inc(sPE.h, 1)
                tok_slot_pv[slot] = sPE.tick()
                E("pe", _pv)
            t_pv_done = tok_slot_pv[sched[-1][0]]
            if qp == 1 and h + 2 < H:
                t_kv_ready[h + 2] = load_head(h + 2, t_pv_done)
            W("dve", sPE, t_pv_done)
            def _recip(e):
                e.reciprocal(SM.rl1[0:1, 0:512], pv_a[64:65, :])
                e.reciprocal(SM.rl1[0:1, 512:1024], pv_b[64:65, :])
                e.tensor_copy(SM.rl16, SM.rl1).then_inc(sV.h, 1)
            t_recip = sV.tick()
            E("dve", _recip)
            W("pe", sV, t_recip)
            def _bc(e):
                e.matmul(bank(4, r0=0, r1=64), SM.ones64, SM.rl16[0:1, 0:512],
                         start=True, stop=True)
                e.matmul(bank(5, r0=0, r1=64), SM.ones64, SM.rl16[0:1, 512:1024],
                         start=True, stop=True).then_inc(sPE.h, 1)
            t_bc = sPE.tick()
            E("pe", _bc)
            W("dve", sPE, t_bc)
            def _ao(e, h=h, qp=qp):
                e.tensor_copy(SM.rlb[:, 0:512], bank(4, r0=0, r1=64))
                e.tensor_copy(SM.rlb[:, 512:1024], bank(5, r0=0, r1=64))
                r = 64 * (h % 2)
                p = h // 2
                e.tensor_tensor(aoT_p(p)[r:r + 64, 512 * qp:512 * qp + 512],
                                pv_a[0:64, :], SM.rlb[:, 0:512], ALU.mult)
                e.tensor_tensor(aoT_p(p)[r:r + 64, 512 * (qp + 2):512 * (qp + 2) + 512],
                                pv_b[0:64, :], SM.rlb[:, 512:1024], ALU.mult).then_inc(sV.h, 1)
            t_aoT = sV.tick()
            E("dve", _ao)
            tok_pv_free = t_aoT
            tok_ao_evac = t_aoT
    t_p2_done = t_aoT
    if KSTAGE <= 3:
        return _finish()

    w1T = at(96 * KB, 8 * 4096, F16)
    wld = [at(160 * KB, 1024, F32), at(164 * KB, 1024, F32)]
    wq1 = at(168 * KB, 1024, F32)
    wq2 = at(172 * KB, 1024, F32)
    wt16 = at(176 * KB, 1024, F16)

    def prep_ternary(w_ap, ntiles, ncols, wT, wT_chunk_cols, scr_o, thr, nthr,
                     mwr, total, wldb, wqa_, wqb_, wt16_, first_pe_waits):
        t_red = [None, None]
        for t in range(ntiles):
            bi = t % 2
            W("pool", sA, t_red[bi])
            tw = sL[bi].tick(16)
            E("pool", lambda e, t=t, bi=bi: e.dma_start(
                out=wldb[bi], in_=w_ap[128 * t:128 * (t + 1), :]).then_inc(sL[bi].h, 16))
            W("act", sL[bi], tw)
            def _wsum(e, t=t, bi=bi):
                gg = ncols // 4
                for g in range(4):
                    k = 4 * t + g
                    tgt = SM.sd if k == 0 else SM.wab8[:, k - 1:k]
                    e.activation(wqa_[:, gg * g:gg * (g + 1)],
                                 wldb[bi][:, gg * g:gg * (g + 1)],
                                 AF.Abs, bias=0.0, scale=1.0,
                                 accum_out=tgt).then_inc(sA.h, 1)
            for _ in range(4):
                t_red[bi] = sA.tick()
            E("act", _wsum)
        W("pool", sA, t_red[0])
        W("pool", sA, t_red[1])
        tw = sL[0].tick(16)
        E("pool", lambda e: e.dma_start(
            out=wldb[0], in_=w_ap[0:128, :]).then_inc(sL[0].h, 16))
        W("act", sL[0], tw)
        t_fl = sA.tick()
        E("act", lambda e, nt=ntiles: e.activation(
            wqa_[:, 0:ncols // 4], wldb[0][:, 0:ncols // 4], AF.Abs, bias=0.0,
            scale=1.0, accum_out=SM.wab8[:, 4 * nt - 1:4 * nt]).then_inc(sA.h, 1))
        W("dve", sA, sA.n)
        t_tot = sV.tick()
        E("dve", lambda e, nt=ntiles: e.reduce_sum(
            SM.wcol, SM.wab8[:, 0:4 * nt], AX.X).then_inc(sV.h, 1))
        W("sp", sV, t_tot)
        t_w = sS.tick(16)
        E("sp", lambda e, o=scr_o: e.dma_start(
            out=sc_scr[o:o + 128], in_=SM.wcol).then_inc(sS.h, 16))
        W("sp", sS, t_w)
        sS.tick(16)
        E("sp", lambda e, o=scr_o: e.dma_start(
            out=SM.wtot[0:1, 0:64], in_=sc_scr[None, o:o + 64]).then_inc(sS.h, 16))
        t_r3 = sS.tick(16)
        E("sp", lambda e, o=scr_o: e.dma_start(
            out=SM.rl1[0:1, 0:64], in_=sc_scr[None, o + 64:o + 128]).then_inc(sS.h, 16))
        W("dve", sS, t_r3)
        t_mean0 = sV.tick()
        def _mean0(e):
            e.tensor_add(SM.rl1[0:1, 64:128], SM.wtot[0:1, 0:64], SM.rl1[0:1, 0:64])
            e.reduce_sum(SM.rowred[0:1, 0:1], SM.rl1[0:1, 64:128],
                         AX.X).then_inc(sV.h, 1)
        E("dve", _mean0)
        W("act", sV, t_mean0)
        t_mean = sA.tick()
        def _mean(e, tot=total):
            e.activation(SM.rowred[0:1, 3:4], SM.rowred[0:1, 0:1], AF.Relu,
                         bias=SM.negep5[0:1, :], scale=1.0 / tot)
            e.activation(SM.rowred[0:1, 1:2], SM.rowred[0:1, 3:4], AF.Relu,
                         bias=SM.eps128[0:1, :], scale=1.0)
            e.activation(SM.rowred[0:1, 2:3], SM.rowred[0:1, 1:2], AF.Relu,
                         bias=0.0, scale=0.5).then_inc(sA.h, 1)
        E("act", _mean)
        W("sp", sA, t_mean)
        t_w2s = sS.tick(16)
        E("sp", lambda e, o=scr_o: e.dma_start(
            out=sc_scr[o + 128:o + 131], in_=SM.rowred[0:1, 0:3]).then_inc(sS.h, 16))
        W("sp", sS, t_w2s)
        sS.tick(16)
        E("sp", lambda e, o=scr_o, thr=thr: e.dma_start(
            out=thr, in_=sc_scr[None, o + 130:o + 131].broadcast_to((128, 1))).then_inc(sS.h, 16))
        t_mw = sS.tick(16)
        E("sp", lambda e, o=scr_o, mwr=mwr: e.dma_start(
            out=mwr, in_=sc_scr[None, o + 129:o + 130].broadcast_to((128, 1))).then_inc(sS.h, 16))
        W("dve", sS, t_mw)
        t_nth0 = sV.tick()
        E("dve", lambda e, thr=thr, nthr=nthr: e.tensor_scalar_mul(
            nthr, thr, -1.0).then_inc(sV.h, 1))
        W("act", sV, t_nth0)
        t_nth = sA.tick()
        E("act", lambda e, nthr=nthr: e.copy(SM.sd, nthr).then_inc(sA.h, 1))
        t_done = None
        first = True
        t_tern2 = [None, None]
        for t in range(ntiles):
            bi = t % 2
            W("pool", sV, t_tern2[bi])
            tw = sL[bi].tick(16)
            E("pool", lambda e, t=t, bi=bi: e.dma_start(
                out=wldb[bi], in_=w_ap[128 * t:128 * (t + 1), :]).then_inc(sL[bi].h, 16))
            W("dve", sL[bi], tw)
            W("dve", sA, t_nth)
            t_tern = sV.tick()
            t_tern2[bi] = t_tern
            def _tern(e, bi=bi, thr=thr, nthr=nthr):
                e.tensor_scalar(wqa_, wldb[bi], thr, None, ALU.is_gt)
                e.tensor_scalar(wqb_, wldb[bi], nthr, None, ALU.is_lt)
                e.tensor_sub(wt16_, wqa_, wqb_).then_inc(sV.h, 1)
            E("dve", _tern)
            W("pe", sV, t_tern)
            pw = first_pe_waits if first else ()
            first = False
            t_done = transpose_waves(
                ncols // 128,
                lambda c, wt16_=wt16_: wt16_[:, 128 * c:128 * (c + 1)],
                lambda e, c, pv, t=t, wT=wT, wcc=wT_chunk_cols: e.tensor_copy(
                    wT[:, wcc * c + 128 * t:wcc * c + 128 * (t + 1)], pv),
                SM.idfs, t_done, pre_pe_waits=pw)
        return t_done

    W("pool", sPE, t_trq_last)
    t_w1T = prep_ternary(w1_in, 32, 1024, w1T, 4096, 256, SM.thr, SM.nthr,
                         SM.mwr, float(HID * D), wld, wq1, wq2, wt16,
                         [("pe", sA, t_exp_last), ("pe", sV, t_p2_done)])
    if KSTAGE <= 4:
        return _finish()

    wout_sb = at(64 * KB, 8 * 1024, BF16)
    xqb = [at(80 * KB, 1024, F32), at(84 * KB, 1024, F32)]
    tz = at(88 * KB, 1024, F32)
    xr = at(92 * KB, 1024, F32)
    junkD = tz

    t_wout = None
    W("pool", sPE, tok_slot_pv["T1"])
    W("pool", sPE, tok_slot_pv["T2"])
    W("pool", sV, t_w1T)
    for c in range(8):
        t_wout = sWc.tick(16)
        E("pool", lambda e, c=c: e.dma_start(
            out=wout_sb[:, 1024 * c:1024 * (c + 1)],
            in_=wout_in[128 * c:128 * (c + 1), :]).then_inc(sWc.h, 16))

    W("pe", sWc, t_wout)
    W("pe", sV, t_p2_done)
    W("pe", sV, t_w1T)
    W("sp", sV, t_w1T)
    tok_zfree = None
    tok_xq_free = [None, None]
    t_x1 = None
    for t in range(NT_OWN):
        bi = t % 2
        W("sp", sV, tok_xq_free[bi])
        t_xq = sH[bi].tick(16)
        E("sp", lambda e, t=t, bi=bi: e.dma_start(
            out=xqb[bi], in_=xq_in[128 * t:128 * (t + 1), :]).then_inc(sH[bi].h, 16))
        W("pe", sV, tok_zfree)
        def _mm_z(e, t=t):
            for nb in (0, 1):
                for p in range(8):
                    ins = e.matmul(bank(nb), aoT_p(p)[:, 128 * t:128 * (t + 1)],
                                   wout_sb[:, 1024 * p + 512 * nb:1024 * p + 512 * (nb + 1)],
                                   start=(p == 0), stop=(p == 7))
            ins.then_inc(sPE.h, 1)
        t_z = sPE.tick()
        E("pe", _mm_z)
        W("dve", sPE, t_z)
        W("dve", sH[bi], t_xq)
        def _ln(e, t=t, bi=bi):
            e.scalar_tensor_tensor(tz, banks(0, 2), 1.0, SM.boutr, ALU.mult, ALU.add)
            e.scalar_tensor_tensor(xr, tz, 1.0, xqb[bi], ALU.mult, ALU.add,
                                   accum_out=SM.s1).then_inc(sV.h, 1)
            e.scalar_tensor_tensor(junkD, xr, 1.0, xr, ALU.mult, ALU.mult,
                                   accum_out=SM.s2).then_inc(sV.h, 1)
        sV.tick()
        t_lnst = sV.tick()
        E("dve", _ln)
        tok_zfree = t_lnst
        W("act", sV, t_lnst)
        t_sm2 = sA.tick()
        def _sm2(e):
            e.activation(SM.sb, SM.s1, AF.Square, bias=0.0,
                         scale=1.0 / 32.0)
            e.copy(SM.sc1, SM.s1)
            e.copy(SM.sc3, SM.s2).then_inc(sA.h, 1)
        E("act", _sm2)
        W("dve", sA, t_sm2)
        t_var = sV.tick()
        def _var(e):
            e.tensor_scalar_mul(SM.sa, SM.sc1, 1.0 / 1024.0)
            e.tensor_sub(SM.sc2, SM.sc3, SM.sb).then_inc(sV.h, 1)
        E("dve", _var)
        W("act", sV, t_var)
        t_rstd = sA.tick()
        def _rstd(e):
            e.activation(SM.sd, SM.sc2, AF.Ln, bias=SM.eps128, scale=1.0 / 1024.0)
            e.activation(SM.se1, SM.sd, AF.Exp, bias=SM.zero128,
                         scale=SM.neghalf).then_inc(sA.h, 1)
        E("act", _rstd)
        W("dve", sA, t_rstd)
        t_x1 = sV.tick()
        E("dve", lambda e, t=t: e.tensor_scalar(
            x1_t(t), xr, SM.sa, SM.se1, ALU.subtract, ALU.mult).then_inc(sV.h, 1))
        tok_xq_free[bi] = t_x1
    t_p3_done = t_x1
    if KSTAGE <= 5:
        return _finish()

    ln1g_r = at(32 * KB, 1024, F32)
    ln1b_r = at(36 * KB, 1024, F32)
    ywork = at(40 * KB, 1024, F32)
    ywork2 = at(44 * KB, 1024, F32)
    yi16 = at(48 * KB, 1024, I16)
    dq16 = at(50 * KB, 1024, F16)
    junk2 = at(52 * KB, 1024, F32)
    dqxT = at(64 * KB, 8 * 2048, F16)

    W("sp", sV, t_p3_done)
    t_l1g = sS.tick(16)
    E("sp", lambda e: e.dma_start(
        out=ln1g_r, in_=ln1g_in[None, :].broadcast_to((128, D))).then_inc(sS.h, 16))
    t_l1b = sS.tick(16)
    E("sp", lambda e: e.dma_start(
        out=ln1b_r, in_=ln1b_in[None, :].broadcast_to((128, D))).then_inc(sS.h, 16))

    W("dve", sV, t_p3_done)
    t_st = None
    for t in range(NT_OWN):
        sV.tick()
        t_st = sV.tick()
        def _st(e, t=t):
            e.reduce_sum(SM.means[:, t:t + 1], x1_t(t), AX.X).then_inc(sV.h, 1)
            e.scalar_tensor_tensor(junk2, x1_t(t), 1.0, x1_t(t), ALU.mult, ALU.mult,
                                   accum_out=SM.rstds[:, t:t + 1]).then_inc(sV.h, 1)
        E("dve", _st)
    W("act", sV, t_st)
    t_msq = sA.tick()
    def _msq(e):
        e.activation(SM.m16b, SM.means, AF.Square, bias=0.0,
                     scale=1.0 / 32.0)
        e.copy(SM.m16c, SM.means)
        e.copy(SM.m16e, SM.rstds).then_inc(sA.h, 1)
    E("act", _msq)
    W("dve", sA, t_msq)
    t_var16 = sV.tick()
    def _var16(e):
        e.tensor_sub(SM.v16, SM.m16e, SM.m16b)
        e.tensor_scalar_mul(SM.m16d, SM.m16c, 1.0 / 1024.0).then_inc(sV.h, 1)
    E("dve", _var16)
    W("act", sV, t_var16)
    t_rstd1 = sA.tick()
    def _rstd1(e):
        e.activation(SM.m16b, SM.v16, AF.Ln, bias=SM.eps128, scale=1.0 / 1024.0)
        e.activation(SM.rstds, SM.m16b, AF.Exp, bias=SM.zero128,
                     scale=SM.neghalf).then_inc(sA.h, 1)
    E("act", _rstd1)

    W("dve", sA, t_rstd1)
    W("dve", sS, t_l1b)
    W("act", sS, sS.n)
    t_dqx_ev = None
    t_q1 = None
    for t in range(NT_OWN):
        def _q1a(e, t=t):
            e.tensor_scalar(ywork, x1_t(t), SM.m16d[:, t:t + 1], SM.rstds[:, t:t + 1],
                            ALU.subtract, ALU.mult)
            e.scalar_tensor_tensor(ywork2, ywork, 1.0, ln1g_r, ALU.mult, ALU.mult)
            e.tensor_add(ywork, ywork2, ln1b_r)
            e.reduce_max(SM.s1, ywork, AX.X,
                         apply_absolute_value=True).then_inc(sV.h, 1)
        t_q1a = sV.tick()
        E("dve", _q1a)
        W("act", sV, t_q1a)
        t_q1b = sA.tick()
        def _q1s(e, t=t):
            e.activation(SM.sd, SM.s1, AF.Relu, bias=SM.negep5, scale=1.0)
            e.activation(SM.se1, SM.sd, AF.Relu, bias=SM.eps128, scale=1.0)
            e.activation(SM.sb, SM.se1, AF.Relu, bias=0.0, scale=1.0 / 127.0)
            e.activation(SM.mwsr[:, t:t + 1], SM.sb, AF.Relu, bias=0.0,
                         scale=SM.mwr).then_inc(sA.h, 1)
        E("act", _q1s)
        W("dve", sA, t_q1b)
        t_q1c = sV.tick()
        E("dve", lambda e: e.reciprocal(SM.s2, SM.se1).then_inc(sV.h, 1))
        W("act", sV, t_q1c)
        t_q1d = sA.tick()
        E("act", lambda e, t=t: e.activation(
            SM.srow[:, t:t + 1], SM.s2, AF.Relu, bias=0.0,
            scale=127.0).then_inc(sA.h, 1))
        W("dve", sA, t_q1d)
        def _q1(e, t=t):
            e.tensor_scalar(ywork2, ywork, SM.srow[:, t:t + 1], None, ALU.mult)
            e.tensor_scalar(junk2, ywork2, 0.0, None, ALU.is_ge)
            e.tensor_scalar(ywork, ywork2, MAGIC_C, MAGIC_C, ALU.add, ALU.subtract)
            e.tensor_scalar(ywork2, junk2, 2.0, 1.0, ALU.mult, ALU.subtract)
            e.scalar_tensor_tensor(junk2, ywork2, SM.kap1, ywork, ALU.mult, ALU.add)
            e.tensor_copy(yi16, junk2)
            e.tensor_copy(dq16, yi16).then_inc(sV.h, 1)
        t_q1 = sV.tick()
        E("dve", _q1)
        W("pe", sV, t_q1)
        t_dqx_ev = transpose_waves(
            8, lambda c: dq16[:, 128 * c:128 * (c + 1)],
            lambda e, c, pv, t=t: e.tensor_copy(
                dqxT[:, 2048 * c + 128 * t:2048 * c + 128 * (t + 1)], pv),
            SM.idfs, t_dqx_ev)
    t_p4a_done = t_dqx_ev
    if KSTAGE <= 6:
        return _finish()

    b1r = at(32 * KB, HID, F16)
    h1f = at(40 * KB, HID, F16)
    gst = [at(48 * KB, HID, F16), at(56 * KB, HID, F16)]

    W("pool", sV, t_q1)
    t_b1 = sWc.tick(16)
    E("pool", lambda e: e.dma_start(
        out=b1r, in_=b1_in[None, :].broadcast_to((128, HID))).then_inc(sWc.h, 16))
    W("dve", sWc, t_b1)

    W("pe", sV, t_w1T)
    W("pe", sV, t_p4a_done)
    tok_gst_free = [None, None]
    t_evnb_hist = {}
    tok_h1_read = None
    t_hdma_last = None
    for t in range(NT_OWN):
        bi = t % 2
        t_evnb = [None] * 8
        for nb in range(8):
            bk = 4 + nb % 2
            if nb >= 2:
                W("pe", sV, t_evnb[nb - 2])
            elif tok_h1_read is not None:
                W("pe", sA, tok_h1_read)
            def _mm1(e, t=t, nb=nb, bk=bk):
                for c in range(8):
                    ins = e.matmul(bank(bk), dqxT[:, 2048 * c + 128 * t:2048 * c + 128 * (t + 1)],
                                   w1T[:, 4096 * c + 512 * nb:4096 * c + 512 * (nb + 1)],
                                   start=(c == 0), stop=(c == 7))
                ins.then_inc(sPE.h, 1)
            t_mm1 = sPE.tick()
            E("pe", _mm1)
            W("dve", sPE, t_mm1)
            if nb == 0 and tok_h1_read is not None:
                W("dve", sA, tok_h1_read)
            def _ev1(e, t=t, nb=nb, bk=bk):
                e.scalar_tensor_tensor(h1f[:, 512 * nb:512 * (nb + 1)], bank(bk),
                                       SM.mwsr[:, t:t + 1],
                                       b1r[:, 512 * nb:512 * (nb + 1)],
                                       ALU.mult, ALU.add).then_inc(sV.h, 1)
            t_evnb[nb] = sV.tick()
            E("dve", _ev1)
        W("act", sV, t_evnb[7])
        W("act", sVs[bi], tok_gst_free[bi])
        t_gelu = sA.tick()
        tok_h1_read = t_gelu
        E("act", lambda e, bi=bi: e.activation(
            gst[bi], h1f, AF.Gelu, bias=SM.zero128,
            scale=SM.one128).then_inc(sA.h, 1))
        W("sp", sA, t_gelu)
        tok_gst_free[bi] = sVs[bi].tick(16)
        E("sp", lambda e, t=t, bi=bi: e.dma_start(
            out=h_scr[t], in_=gst[bi]).then_inc(sVs[bi].h, 16))

    w2T = at(96 * KB, 32 * 1024, F16)
    wld2 = [at(160 * KB, 2048, F32), at(160 * KB, 2048, F32)]
    wqa2 = at(168 * KB, 2048, F32)
    wqb2 = at(176 * KB, 2048, F32)
    wt16b = at(184 * KB, 2048, F16)

    def prep_w2():
        W("pool", sV, t_w1T)
        t_red = [None, None]
        for t in range(8):
            for half in range(2):
                bi = (2 * t + half) % 2
                W("pool", sA, t_red[0])
                W("pool", sA, t_red[1])
                tw = sL[bi].tick(16)
                E("pool", lambda e, t=t, bi=bi, half=half: e.dma_start(
                    out=wld2[bi], in_=w2_in[128 * t:128 * (t + 1),
                                            2048 * half:2048 * (half + 1)]).then_inc(sL[bi].h, 16))
                W("act", sL[bi], tw)
                def _wsum2(e, t=t, bi=bi, half=half):
                    for g in range(4):
                        k = 4 * (2 * t + half) + g
                        tgt = SM.sd if k == 0 else SM.wab8[:, k - 1:k]
                        e.activation(wqa2[:, 512 * g:512 * (g + 1)],
                                     wld2[bi][:, 512 * g:512 * (g + 1)],
                                     AF.Abs, bias=0.0, scale=1.0,
                                     accum_out=tgt).then_inc(sA.h, 1)
                for _ in range(4):
                    t_red[0] = t_red[1] = sA.tick()
                E("act", _wsum2)
        W("pool", sA, t_red[0])
        tw = sL[0].tick(16)
        E("pool", lambda e: e.dma_start(
            out=wld2[0], in_=w2_in[0:128, 0:2048]).then_inc(sL[0].h, 16))
        W("act", sL[0], tw)
        t_fl2 = sA.tick()
        E("act", lambda e: e.activation(
            wqa2[:, 0:512], wld2[0][:, 0:512], AF.Abs, bias=0.0, scale=1.0,
            accum_out=SM.wab8[:, 63:64]).then_inc(sA.h, 1))
        W("dve", sA, sA.n)
        t_tot = sV.tick()
        E("dve", lambda e: e.reduce_sum(SM.wcol, SM.wab8[:, 0:64],
                                        AX.X).then_inc(sV.h, 1))
        o = 384
        W("sp", sV, t_tot)
        t_w = sS.tick(16)
        E("sp", lambda e, o=o: e.dma_start(out=sc_scr[o:o + 128],
                                           in_=SM.wcol).then_inc(sS.h, 16))
        W("sp", sS, t_w)
        sS.tick(16)
        E("sp", lambda e, o=o: e.dma_start(out=SM.wtot[0:1, 0:64],
                                           in_=sc_scr[None, o:o + 64]).then_inc(sS.h, 16))
        t_r3 = sS.tick(16)
        E("sp", lambda e, o=o: e.dma_start(out=SM.rl1[0:1, 0:64],
                                           in_=sc_scr[None, o + 64:o + 128]).then_inc(sS.h, 16))
        W("dve", sS, t_r3)
        t_mean0 = sV.tick()
        def _mean0(e):
            e.tensor_add(SM.rl1[0:1, 64:128], SM.wtot[0:1, 0:64], SM.rl1[0:1, 0:64])
            e.reduce_sum(SM.rowred[0:1, 0:1], SM.rl1[0:1, 64:128],
                         AX.X).then_inc(sV.h, 1)
        E("dve", _mean0)
        W("act", sV, t_mean0)
        t_mean = sA.tick()
        def _mean(e):
            e.activation(SM.rowred[0:1, 3:4], SM.rowred[0:1, 0:1], AF.Relu,
                         bias=SM.negep5[0:1, :], scale=1.0 / (D * HID))
            e.activation(SM.rowred[0:1, 1:2], SM.rowred[0:1, 3:4], AF.Relu,
                         bias=SM.eps128[0:1, :], scale=1.0)
            e.activation(SM.rowred[0:1, 2:3], SM.rowred[0:1, 1:2], AF.Relu,
                         bias=0.0, scale=0.5).then_inc(sA.h, 1)
        E("act", _mean)
        W("sp", sA, t_mean)
        t_w2s = sS.tick(16)
        E("sp", lambda e, o=o: e.dma_start(out=sc_scr[o + 128:o + 131],
                                           in_=SM.rowred[0:1, 0:3]).then_inc(sS.h, 16))
        W("sp", sS, t_w2s)
        sS.tick(16)
        E("sp", lambda e, o=o: e.dma_start(
            out=SM.thr2,
            in_=sc_scr[None, o + 130:o + 131].broadcast_to((128, 1))).then_inc(sS.h, 16))
        t_mw = sS.tick(16)
        E("sp", lambda e, o=o: e.dma_start(
            out=SM.mwr2,
            in_=sc_scr[None, o + 129:o + 130].broadcast_to((128, 1))).then_inc(sS.h, 16))
        W("dve", sS, t_mw)
        t_nth0 = sV.tick()
        E("dve", lambda e: e.tensor_scalar_mul(SM.nthr2, SM.thr2, -1.0).then_inc(sV.h, 1))
        W("act", sV, t_nth0)
        t_nth = sA.tick()
        E("act", lambda e: e.copy(SM.sd, SM.nthr2).then_inc(sA.h, 1))
        t_done = None
        first = True
        t_tern2 = [None, None]
        for t in range(8):
            for half in range(2):
                bi = (2 * t + half) % 2
                W("pool", sV, t_tern2[bi])
                tw = sL[bi].tick(16)
                E("pool", lambda e, t=t, bi=bi, half=half: e.dma_start(
                    out=wld2[bi], in_=w2_in[128 * t:128 * (t + 1),
                                            2048 * half:2048 * (half + 1)]).then_inc(sL[bi].h, 16))
                W("dve", sL[bi], tw)
                W("dve", sA, t_nth)
                t_tern = sV.tick()
                t_tern2[0] = t_tern
                t_tern2[1] = t_tern
                def _tern(e, bi=bi):
                    e.tensor_scalar(wqa2, wld2[bi], SM.thr2, None, ALU.is_gt)
                    e.tensor_scalar(wqb2, wld2[bi], SM.nthr2, None, ALU.is_lt)
                    e.tensor_sub(wt16b, wqa2, wqb2).then_inc(sV.h, 1)
                E("dve", _tern)
                W("pe", sV, t_tern)
                first = False
                t_done = transpose_waves(
                    16, lambda c: wt16b[:, 128 * c:128 * (c + 1)],
                    lambda e, c, pv, t=t, half=half: e.tensor_copy(
                        w2T[:, 1024 * (16 * half + c) + 128 * t:
                            1024 * (16 * half + c) + 128 * (t + 1)], pv),
                    SM.idfs, t_done)
        return t_done

    t_w2T = prep_w2()
    if KSTAGE <= 7:
        return _finish()

    ln2g_r = at(32 * KB, HID, F16)
    ln2b_r = at(40 * KB, HID, F16)
    hld = [at(48 * KB, HID, F16), at(56 * KB, HID, F16)]
    hwork = at(64 * KB, HID, F32)
    dqhT = at(64 * KB, HID, F16)
    scrH16 = at(80 * KB, HID, F16)
    scrHi = at(80 * KB, HID, I16)
    hq16 = at(88 * KB, HID, F16)
    b2r = at(160 * KB, 1024, F32)
    outst = [at(164 * KB, 1024, F32), at(168 * KB, 1024, F32)]
    tzc = at(172 * KB, 1024, F32)
    xrc = at(176 * KB, 1024, F32)
    qB = at(180 * KB, 2048, F32)
    junkc = tzc

    W("sp", sA, tok_h1_read)
    W("sp", sVs[0], sVs[0].n)
    W("sp", sVs[1], sVs[1].n)
    W("sp", sV, t_w2T)
    W("pool", sA, tok_h1_read)
    W("pool", sVs[0], sVs[0].n)
    W("pool", sVs[1], sVs[1].n)
    t_l2g = sWc.tick(16)
    E("pool", lambda e: e.dma_start(
        out=ln2g_r, in_=ln2g_in[None, :].broadcast_to((128, HID))).then_inc(sWc.h, 16))
    t_l2b = sWc.tick(16)
    E("pool", lambda e: e.dma_start(
        out=ln2b_r, in_=ln2b_in[None, :].broadcast_to((128, HID))).then_inc(sWc.h, 16))
    t_b2 = sS.tick(16)
    E("sp", lambda e: e.dma_start(
        out=b2r, in_=b2_in[None, :].broadcast_to((128, D))).then_inc(sS.h, 16))

    W("act", sS, sS.n)
    tok_hld_free = [None, None]
    tok_hld_free2 = [None, None]
    tok_out_free = [None, None]
    t_evh_carry = [None]
    t_ep_prev = None
    t_out_final = None
    W("pe", sV, t_w2T)
    for t in range(NT_OWN):
        bi = t % 2
        W("sp", sV, tok_hld_free2[bi])
        if t == 0:
            W("sp", sVs[0], sVs[0].n)
            W("sp", sVs[1], sVs[1].n)
        t_h = sH[bi].tick(16)
        E("sp", lambda e, t=t, bi=bi: e.dma_start(
            out=hld[bi], in_=h_scr[t]).then_inc(sH[bi].h, 16))
        W("dve", sH[bi], t_h)
        W("dve", sWc, t_l2b)
        W("dve", sS, t_b2)
        sV.tick()
        t_st2 = sV.tick()
        def _st2(e, bi=bi):
            e.scalar_tensor_tensor(scrH16, hld[bi], 1.0, hld[bi], ALU.mult, ALU.mult,
                                   accum_out=SM.s2).then_inc(sV.h, 1)
            e.reduce_sum(SM.s1, hld[bi], AX.X).then_inc(sV.h, 1)
        E("dve", _st2)
        W("act", sV, t_st2)
        t_sm2c = sA.tick()
        def _sm2c(e):
            e.activation(SM.sb, SM.s1, AF.Square, bias=0.0,
                         scale=1.0 / 64.0)
            e.copy(SM.sc1, SM.s1)
            e.copy(SM.sc3, SM.s2).then_inc(sA.h, 1)
        E("act", _sm2c)
        W("dve", sA, t_sm2c)
        t_varc = sV.tick()
        def _varc(e):
            e.tensor_scalar_mul(SM.sa, SM.sc1, 1.0 / HID)
            e.tensor_sub(SM.sc2, SM.sc3, SM.sb).then_inc(sV.h, 1)
        E("dve", _varc)
        W("act", sV, t_varc)
        t_rstd2 = sA.tick()
        def _rstd2(e):
            e.activation(SM.sd, SM.sc2, AF.Ln, bias=SM.eps128, scale=1.0 / HID)
            e.activation(SM.se1, SM.sd, AF.Exp, bias=SM.zero128,
                         scale=SM.neghalf).then_inc(sA.h, 1)
        E("act", _rstd2)
        W("dve", sA, t_rstd2)
        t_ynorm = sV.tick()
        E("dve", lambda e, bi=bi: e.tensor_scalar(
            hwork, hld[bi], SM.sa, SM.se1, ALU.subtract, ALU.mult).then_inc(sV.h, 1))
        tok_hld_free[bi] = None
        tok_hld_free2[bi] = t_ynorm
        def _q2a(e, t=t):
            e.scalar_tensor_tensor(scrH16, hwork, 1.0, ln2g_r, ALU.mult, ALU.mult)
            e.tensor_add(hwork, scrH16, ln2b_r)
            e.reduce_max(SM.s1, hwork, AX.X,
                         apply_absolute_value=True).then_inc(sV.h, 1)
        t_q2a = sV.tick()
        E("dve", _q2a)
        W("act", sV, t_q2a)
        t_q2b = sA.tick()
        def _q2s(e):
            e.activation(SM.sd, SM.s1, AF.Relu, bias=SM.negep5, scale=1.0)
            e.activation(SM.se1, SM.sd, AF.Relu, bias=SM.eps128, scale=1.0)
            e.activation(SM.sd, SM.se1, AF.Relu, bias=0.0, scale=1.0 / 127.0)
            e.activation(SM.sa, SM.sd, AF.Relu, bias=0.0,
                         scale=SM.mwr2).then_inc(sA.h, 1)
        E("act", _q2s)
        W("dve", sA, t_q2b)
        t_q2c = sV.tick()
        E("dve", lambda e: e.reciprocal(SM.s2, SM.se1).then_inc(sV.h, 1))
        W("act", sV, t_q2c)
        t_q2d = sA.tick()
        E("act", lambda e: e.activation(
            SM.sb, SM.s2, AF.Relu, bias=0.0,
            scale=127.0).then_inc(sA.h, 1))
        W("dve", sA, t_q2d)
        def _q2(e, t=t):
            for h2 in (0, 1):
                sl = slice(2048 * h2, 2048 * (h2 + 1))
                e.tensor_scalar(qB, hwork[:, sl], SM.sb, None, ALU.mult)
                e.tensor_scalar(scrH16[:, sl], qB, 0.0, None, ALU.is_ge)
                e.tensor_scalar(hwork[:, sl], qB, MAGIC_C, MAGIC_C, ALU.add, ALU.subtract)
                e.tensor_scalar(qB, scrH16[:, sl], 2.0, 1.0, ALU.mult, ALU.subtract)
                e.scalar_tensor_tensor(qB, qB, SM.kap1, hwork[:, sl], ALU.mult, ALU.add)
                e.tensor_copy(scrHi[:, sl], qB)
                ins = e.tensor_copy(hq16[:, sl], scrHi[:, sl])
            ins.then_inc(sV.h, 1)
        t_q2 = sV.tick()
        E("dve", _q2)
        W("pe", sV, t_q2)
        t_evh = transpose_waves(
            32, lambda c: hq16[:, 128 * c:128 * (c + 1)],
            lambda e, c, pv: e.tensor_copy(dqhT[:, 128 * c:128 * (c + 1)], pv),
            SM.idfs, t_evh_carry[0])
        t_evh_carry[0] = t_evh
        W("pe", sV, t_evh)
        W("pe", sV, t_ep_prev)
        def _mm2(e):
            for nb in (0, 1):
                for c in range(32):
                    ins = e.matmul(bank(6 + nb), dqhT[:, 128 * c:128 * (c + 1)],
                                   w2T[:, 1024 * c + 512 * nb:1024 * c + 512 * (nb + 1)],
                                   start=(c == 0), stop=(c == 31))
            ins.then_inc(sPE.h, 1)
        t_mm2 = sPE.tick()
        E("pe", _mm2)
        W("dve", sPE, t_mm2)
        def _ep(e, t=t, bi=bi):
            e.scalar_tensor_tensor(tzc, banks(6, 2), SM.sa, b2r, ALU.mult, ALU.add)
            e.scalar_tensor_tensor(xrc, tzc, 1.0, x1_t(t), ALU.mult, ALU.add,
                                   accum_out=SM.s1).then_inc(sV.h, 1)
            e.scalar_tensor_tensor(junkc, xrc, 1.0, xrc, ALU.mult, ALU.mult,
                                   accum_out=SM.s2).then_inc(sV.h, 1)
        sV.tick()
        t_ep = sV.tick()
        t_ep_prev = t_ep
        E("dve", _ep)
        W("act", sV, t_ep)
        t_sm2e = sA.tick()
        def _sm2e(e):
            e.activation(SM.sb, SM.s1, AF.Square, bias=0.0,
                         scale=1.0 / 32.0)
            e.copy(SM.sc1, SM.s1)
            e.copy(SM.sc3, SM.s2).then_inc(sA.h, 1)
        E("act", _sm2e)
        W("dve", sA, t_sm2e)
        t_vare = sV.tick()
        def _vare(e):
            e.tensor_scalar_mul(SM.sa, SM.sc1, 1.0 / 1024.0)
            e.tensor_sub(SM.sc2, SM.sc3, SM.sb).then_inc(sV.h, 1)
        E("dve", _vare)
        W("act", sV, t_vare)
        t_rstd3 = sA.tick()
        def _rstd3(e):
            e.activation(SM.sd, SM.sc2, AF.Ln, bias=SM.eps128, scale=1.0 / 1024.0)
            e.activation(SM.se1, SM.sd, AF.Exp, bias=SM.zero128,
                         scale=SM.neghalf).then_inc(sA.h, 1)
        E("act", _rstd3)
        W("dve", sA, t_rstd3)
        W("dve", sOs[bi], tok_out_free[bi])
        t_ost = sV.tick()
        E("dve", lambda e, bi=bi: e.tensor_scalar(
            outst[bi], xrc, SM.sa, SM.se1, ALU.subtract, ALU.mult).then_inc(sV.h, 1))
        W("sp", sV, t_ost)
        tok_out_free[bi] = sOs[bi].tick(16)
        E("sp", lambda e, t=t, bi=bi: e.dma_start(
            out=out_hbm[128 * t:128 * (t + 1), :], in_=outst[bi]).then_inc(sOs[bi].h, 16))

    W("sp", sOs[0], sOs[0].n)
    W("sp", sOs[1], sOs[1].n)
    return _finish()


_NC_CACHE = None


def kernel(**inputs):
    global _NC_CACHE
    xfull = np.asarray(inputs["L0_attn_x"], dtype=np.float32)
    wq = np.ascontiguousarray(np.asarray(inputs["L0_attn_w_q"], np.float32))
    wkv = np.ascontiguousarray(np.asarray(inputs["L0_attn_w_kv"], np.float32))
    wout = np.ascontiguousarray(np.asarray(inputs["L0_attn_w_out"], np.float32))
    bout = np.ascontiguousarray(np.asarray(inputs["L0_attn_b_out"], np.float32))
    scale = np.ascontiguousarray(np.asarray(inputs["L0_attn_scale"], np.float32).reshape(H))
    ln1g = np.ascontiguousarray(np.asarray(inputs["L0_ffn_ln1_g"], np.float32))
    ln1b = np.ascontiguousarray(np.asarray(inputs["L0_ffn_ln1_b"], np.float32))
    w1 = np.ascontiguousarray(np.asarray(inputs["L0_ffn_w1"], np.float32))
    b1 = np.ascontiguousarray(np.asarray(inputs["L0_ffn_b1"], np.float32))
    ln2g = np.ascontiguousarray(np.asarray(inputs["L0_ffn_ln2_g"], np.float32))
    ln2b = np.ascontiguousarray(np.asarray(inputs["L0_ffn_ln2_b"], np.float32))
    w2 = np.ascontiguousarray(np.asarray(inputs["L0_ffn_w2"], np.float32))
    b2 = np.ascontiguousarray(np.asarray(inputs["L0_ffn_b2"], np.float32))
    idf = np.eye(128, dtype=np.float16)
    idb = np.eye(128, dtype=np.float32).astype(ml_dtypes.bfloat16)

    if _NC_CACHE is None:
        _NC_CACHE = build_program()
    nc = _NC_CACHE

    in_maps = []
    for c in range(8):
        b = c // 4
        q0 = (c % 4) * NTOK
        in_maps.append({
            "x": np.ascontiguousarray(xfull[b]),
            "xq": np.ascontiguousarray(xfull[b, q0:q0 + NTOK]),
            "wq": wq, "wkv": wkv, "wout": wout, "bout": bout, "scale": scale,
            "ln1g": ln1g, "ln1b": ln1b, "w1": w1, "b1": b1,
            "ln2g": ln2g, "ln2b": ln2b, "w2": w2, "b2": b2,
            "idf": idf, "idb": idb,
        })
    res = run_bass_kernel_spmd(nc, in_maps, core_ids=list(range(8)))
    out = np.empty((B, S, D), dtype=np.float32)
    for c in range(8):
        b = c // 4
        q0 = (c % 4) * NTOK
        out[b, q0:q0 + NTOK] = res.results[c]["out"]
    return out
